# version 02
# baseline (speedup 1.0000x reference)
import math
import numpy as np
import ml_dtypes
from contextlib import ExitStack

import concourse.bass as bass
import concourse.mybir as mybir
from concourse.bass_utils import run_bass_kernel_spmd

F32 = mybir.dt.float32
BF16 = mybir.dt.bfloat16
I32 = mybir.dt.int32
I8 = mybir.dt.int8
AF = mybir.ActivationFunctionType
ALU = mybir.AluOpType
AX = mybir.AxisListType

NCORE = 8
CONST_VALUES = [1e-5, -0.5, 256.5, -1.0, 1.0, 1e-4, 1.0 / 256, -256.0, 0.5, 2.0,
                -4095.5, 1e-6, 1.0 / 1024, 127.0, 1.0 / 127, -1e9, 0.7,
                128.0 ** -0.5, 64.0 ** -0.25, 1.0 / 64, -1e-30, 1e-30, 0.0625,
                -0.0625, 8192.0, -8192.0, 2048.0, 16.0, -16.0, 1e9, 0.25, 4.0]
B, S, D = 2, 8192, 1024
TPC = S // 4
NCH_L = TPC // 128
GROUPS = [[0, 1, 2, 3], [4, 5, 6, 7]]
INNER = 4096
BF = ml_dtypes.bfloat16

def CP(e, dst, src):
    return e.copy(dst, src) if isinstance(e, bass.BassScalarEngine) else e.tensor_copy(dst, src)
HDK_SC = 128.0 ** -0.5


class Em:
    ENGINES = ("t", "v", "s", "p", "q")

    RING = 6

    def __init__(self, nc):
        self.nc = nc
        self.stk = ExitStack()
        self.sem, self.cnt = {}, {}
        for k in ("t", "v", "s", "p", "q", "C"):
            self.sem[k] = self.stk.enter_context(nc.semaphore(f"sm_{k}"))
            self.cnt[k] = 0
        for q in ("Q", "P", "S"):
            for i in range(self.RING):
                k = f"{q}{i}"
                self.sem[k] = self.stk.enter_context(nc.semaphore(f"sm_{k}"))
                self.cnt[k] = 0
            self.cnt[q] = 0
        self.ops = {e: [] for e in self.ENGINES}
        self.writer, self.readers = {}, {}
        self.n_ops = 0

    def _collect(self, reads, writes):
        waits = {}
        def dep(tok):
            if tok is not None:
                sk, v = tok
                if v > waits.get(sk, 0):
                    waits[sk] = v
        for k in reads:
            dep(self.writer.get(k))
        for k in writes:
            dep(self.writer.get(k))
            for t in self.readers.get(k, ()):
                dep(t)
        return waits

    def _record(self, tok, reads, writes):
        for k in reads:
            self.readers.setdefault(k, []).append(tok)
        for k in writes:
            self.writer[k] = tok
            self.readers[k] = []

    def op(self, eng, fn, reads=(), writes=()):
        waits = self._collect(list(reads) + ["__const"], writes)
        self.cnt[eng] += 1
        tok = (eng, self.cnt[eng])
        self._record(tok, reads, writes)
        self.ops[eng].append((waits, fn, eng, 1))
        self.n_ops += 1
        return tok

    def dma(self, q, fn, reads=(), writes=()):
        qeng = {"Q": "q", "P": "p", "S": "s"}[q]
        waits = self._collect(reads, writes)
        sk = f"{q}{self.cnt[q] % self.RING}"
        if self.cnt[sk] > waits.get(sk, 0):
            waits[sk] = self.cnt[sk]
        self.cnt[q] += 1
        self.cnt[sk] += 16
        tok = (sk, self.cnt[sk])
        self._record(tok, reads, writes)
        self.ops[qeng].append((waits, fn, sk, 16))
        self.n_ops += 1
        return tok

    def cc(self, fn, reads=(), writes=()):
        waits = self._collect(reads, writes)
        self.cnt["C"] += 1
        tok = ("C", self.cnt["C"])
        self._record(tok, reads, writes)
        self.ops["p"].append((waits, fn, "C", None))
        self.n_ops += 1
        return tok

    def mm_group(self, fns, reads=(), writes=()):
        waits = self._collect(list(reads) + ["__const"], writes)
        self.cnt["t"] += 1
        tok = ("t", self.cnt["t"])
        self._record(tok, reads, writes)
        for i, fn in enumerate(fns):
            last = i == len(fns) - 1
            self.ops["t"].append((waits if i == 0 else {}, fn, "t" if last else None, 1))
            self.n_ops += 1
        return tok

    def finalize(self):
        nc = self.nc
        engmap = {"t": "tensor", "v": "vector", "s": "scalar", "p": "gpsimd", "q": "sync"}
        final_waits = {"q": [f"Q{i}" for i in range(self.RING)],
                       "p": [f"P{i}" for i in range(self.RING)] + ["C"],
                       "s": [f"S{i}" for i in range(self.RING)]}
        with nc.Block() as block:
            for e in self.ENGINES:
                ops = self.ops[e]

                def make_body(ops=ops, e=e):
                    def body(eng):
                        waited = {}
                        for waits, fn, inc_key, inc_val in ops:
                            for sk, v in sorted(waits.items()):
                                if v > waited.get(sk, 0):
                                    eng.wait_ge(self.sem[sk], v)
                                    waited[sk] = v
                            ins = fn(eng)
                            if inc_key is not None:
                                if inc_val is None:
                                    ins.then_inc(self.sem[inc_key])
                                else:
                                    ins.then_inc(self.sem[inc_key], inc_val)
                        for q in final_waits.get(e, ()):
                            if self.cnt[q] > waited.get(q, 0):
                                eng.wait_ge(self.sem[q], self.cnt[q])
                    return body

                getattr(block, engmap[e])(make_body())
        self.stk.close()


class Arena:

    def __init__(self, nc, em, nbytes):
        self.nc, self.nbytes, self.n = nc, nbytes, 0
        self.back = em.stk.enter_context(
            nc.sbuf_tensor("arena", [128, nbytes], mybir.dt.uint8))
        self.base = nc.lookup_mloc(self.back).addr

    def at(self, off, shape, dtype, name=None):
        per = int(np.prod(shape[1:])) * mybir.dt.size(dtype)
        assert off % 32 == 0 and off + per <= self.nbytes, (off, per)
        self.n += 1
        name = name or f"ar{self.n}"
        h = self.nc.virtual_sbuf_tensor(name, list(shape), dtype)
        self.nc._tensor(h.name, list(shape), dtype, type="SB")
        mloc = self.nc.lookup_mloc(h)
        mloc.allocated = not self.nc.target_bir_lowering
        mloc.addr = self.base + off
        return h


def make_in_maps(inputs):
    g = lambda k: np.asarray(inputs[k], np.float32)
    x = g("L0_attn_a_x")
    w_gate, w_q, w_k, w_v = g("L0_attn_a_w_gate"), g("L0_attn_a_w_q"), g("L0_attn_a_w_k"), g("L0_attn_a_w_v")
    w_gk1, w_gk2, b_gk = g("L0_attn_a_w_gk1"), g("L0_attn_a_w_gk2"), g("L0_attn_a_b_gk")
    w_ks, w_vs = g("L0_attn_a_w_ks"), g("L0_attn_a_w_vs")
    w_gk1s, w_gk2s, b_gks = g("L0_attn_a_w_gk1s"), g("L0_attn_a_w_gk2s"), g("L0_attn_a_b_gks")
    w_g, w_o, norm_w = g("L0_attn_a_w_g"), g("L0_attn_a_w_o"), g("L0_attn_a_norm_w")
    W_qkv, b_qkv = g("L0_attn_b_W_qkv"), g("L0_attn_b_b_qkv")
    W_proj, b_proj, pm = g("L0_attn_b_W_proj"), g("L0_attn_b_b_proj"), g("L0_attn_b_proj_mat")
    fw_gate, fw_up, fw_down = g("L0_ffn_w_gate"), g("L0_ffn_w_up"), g("L0_ffn_w_down")

    C = np.ascontiguousarray
    ii, jj = np.meshgrid(np.arange(128), np.arange(128), indexing="ij")
    tri = (ii <= jj).astype(BF)
    tri_strict = (ii < jj).astype(np.float32)
    tri_incl = (ii <= jj).astype(np.float32)
    ident = np.eye(128, dtype=BF)
    lgk = ((ii <= jj).astype(np.float32) * (-1.0 / 16.0)).astype(BF)
    cv = np.zeros(64, np.float32)
    cv[:len(CONST_VALUES)] = CONST_VALUES
    constv = np.tile(cv[None, :], (128, 1))
    sel = np.zeros((8, 16, 128), np.float32)
    for p in range(8):
        sel[p, 2 * p, 0:64] = 1.0
        sel[p, 2 * p + 1, 64:128] = 1.0

    in_maps = []
    for c in range(NCORE):
        b, h = c // 4, c % 4
        ks = slice(h * 128, (h + 1) * 128)
        vs = slice(h * 256, (h + 1) * 256)
        m = {
            "x_my": C(x[b, (c % 4) * TPC:(c % 4 + 1) * TPC]),
            "wgate": C(w_gate),
            "wq_h": C(w_q[:, ks]), "wks_h": C(w_ks[:, ks]),
            "wk_h": C(w_k[:, :, ks]), "wv_h": C(w_v[:, :, vs]), "wvs_h": C(w_vs[:, vs]),
            "wgk1": C(np.concatenate([w_gk1[i] for i in range(4)] + [w_gk1s], axis=1)),
            "w2aug": C(np.stack(
                [np.concatenate([w_gk2[i][:, ks], b_gk[i:i + 1, ks]], axis=0) for i in range(4)]
                + [np.concatenate([w_gk2s[:, ks], b_gks[None, ks]], axis=0)])),
            "wg_h": C(w_g[:, vs]), "wo_h": C(w_o[vs, :]),
            "norm_w2": C(norm_w.reshape(2, 128)),
            "wqkvT": C(W_qkv.T), "bqkv": C(b_qkv.reshape(1, 3072)),
            "wprojT": C(W_proj.T), "bproj": C(b_proj.reshape(1, D)),
            "pmT": C(pm.T),
            "fgT_my": C(fw_gate.T[c * 128:(c + 1) * 128]),
            "fuT_my": C(fw_up.T[c * 128:(c + 1) * 128]),
            "fdT_my": C(fw_down.T[c * 512:(c + 1) * 512]),
            "tri": tri, "tri_strict": tri_strict, "tri_incl": tri_incl, "ident": ident, "lgk": lgk,
            "sel": sel.astype(BF),
            "constv": constv,
        }
        in_maps.append(m)
    return in_maps


IN_SPECS = [
    ("x_my", [TPC, D], F32),
    ("wgate", [D, 4], F32), ("wq_h", [D, 128], F32), ("wks_h", [D, 128], F32),
    ("wk_h", [4, D, 128], F32), ("wv_h", [4, D, 256], F32), ("wvs_h", [D, 256], F32),
    ("wgk1", [D, 80], F32), ("w2aug", [5, 17, 128], F32),
    ("wg_h", [D, 256], F32), ("wo_h", [256, D], F32), ("norm_w2", [2, 128], F32),
    ("wqkvT", [D, 3072], F32), ("bqkv", [1, 3072], F32),
    ("wprojT", [D, D], F32), ("bproj", [1, D], F32), ("pmT", [64, 64], F32),
    ("fgT_my", [128, INNER], F32), ("fuT_my", [128, INNER], F32), ("fdT_my", [512, D], F32),
    ("tri", [128, 128], BF16), ("tri_strict", [128, 128], F32), ("tri_incl", [128, 128], F32),
    ("ident", [128, 128], BF16), ("lgk", [128, 128], BF16),
    ("sel", [8, 16, 128], BF16),
    ("constv", [128, 64], F32),
]


def build_program(dbg=()):
    nc = bass.Bass("TRN2", target_bir_lowering=False)
    em = Em(nc)
    A = Arena(nc, em, 180 * 1024)

    din = {}
    for name, shape, dt in IN_SPECS:
        din[name] = nc.dram_tensor(name, shape, dt, kind="ExternalInput").ap()
    OUT = nc.dram_tensor("out", [TPC, D], F32, kind="ExternalOutput").ap()

    ag1_in = nc.dram_tensor("ag1_in", [NCH_L * 8 * 128, 128], BF16).ap()
    ag1_out = nc.dram_tensor("ag1_out", [4 * NCH_L * 8 * 128, 128], BF16).ap()
    ag2_in = nc.dram_tensor("ag2_in", [4, TPC], F32).ap()
    ag2_out = nc.dram_tensor("ag2_out", [16, TPC], F32).ap()
    rt_b = nc.dram_tensor("rt_b", [1, 1024], F32).ap()
    rs_in = nc.dram_tensor("rs_in", [4 * NCH_L * 8 * 128, 128], BF16).ap()
    rs_out = nc.dram_tensor("rs_out", [NCH_L * 8 * 128, 128], BF16).ap()
    tern_in = {w: nc.dram_tensor(f"tern_{w}_in", [128 if w != "d" else 512, INNER if w != "d" else D], BF16).ap() for w in ("g", "u", "d")}
    tern_out = {w: nc.dram_tensor(f"tern_{w}_out", [D if w != "d" else INNER, INNER if w != "d" else D], BF16).ap() for w in ("g", "u", "d")}
    asum_in = nc.dram_tensor("asum_in", [1, 4], F32).ap()
    asum_out = nc.dram_tensor("asum_out", [1, 4], F32).ap()
    kv_in = nc.dram_tensor("kv_in", [128, 520], F32).ap()
    kv_out = nc.dram_tensor("kv_out", [128, 520], F32).ap()
    wqkvT_bf = nc.dram_tensor("wqkvT_bf", [D, 3072], BF16).ap()
    wprojT_bf = nc.dram_tensor("wprojT_bf", [D, D], BF16).ap()
    resid_park = nc.dram_tensor("resid_park", [128, NCH_L * D], F32).ap()
    uq_park = nc.dram_tensor("uq_park", [128, NCH_L * INNER], I8).ap()

    dbg_out = {}
    for name, shape, dt in dbg:
        dbg_out[name] = nc.dram_tensor(name, shape, dt, kind="ExternalOutput").ap()

    constv_s = em.stk.enter_context(nc.sbuf_tensor("constv_s", [128, 64], F32))
    em.dma("Q", lambda e: e.dma_start(out=constv_s[:, :], in_=din["constv"]), writes=["__const"])
    for _i, _v in enumerate(CONST_VALUES):
        nc.const_aps.register(F32, float(_v), constv_s[:, _i:_i + 1])
        nc.const_aps.register(BF16, float(_v), constv_s[:, _i:_i + 1])

    PS = [em.stk.enter_context(nc.psum_tensor(f"ps{i}", [128, 512], F32)) for i in range(6)]
    PB = [em.stk.enter_context(nc.psum_tensor(f"pb{i}", [128, 1024], BF16)) for i in range(2)]

    Pd = {}
    def S_(name, shape, dt):
        Pd[name] = em.stk.enter_context(nc.sbuf_tensor(name + "_sb", shape, dt))
        return Pd[name]

    tri = S_("tri", [128, 128], BF16)
    triX = S_("triX", [128, 128], F32)
    triI = S_("triI", [128, 128], F32)
    ident = S_("ident", [128, 128], BF16)
    identf = S_("identf", [128, 128], F32)
    lgk = S_("lgk", [128, 128], BF16)
    kflag = S_("kflag", [128, 1], F32)
    u_all = S_("u_all", [128, 256], F32)
    keep_all = S_("keep_all", [128, 256], F32)
    invsx = S_("invsx", [128, NCH_L], F32)
    normw = S_("normw", [128, 2], F32)
    small = S_("small", [128, 64], F32)
    row1 = S_("row1", [1, 2048], F32)
    orow_f = S_("orow_f", [1, 128], F32)
    onesf = S_("onesf", [128, 1], F32)
    ktmpi = S_("ktmpi", [128, 1], I32)
    S_("b_pm2", [128, 128], BF16)
    S_("b_bq", [128, 24], F32)
    S_("b_bpr", [1, 1024], BF16)
    S_("b_bqv", [1, 1024], BF16)
    S_("b_onr", [1, 256], BF16)
    S_("b_sel", [16, 8 * 128], BF16)

    for nm, key in [("tri", "tri"), ("tri_strict", "triX"), ("tri_incl", "triI"), ("ident", "ident"), ("lgk", "lgk")]:
        em.dma("Q", lambda e, nm=nm, key=key: e.dma_start(out=Pd[key][:, :], in_=din[nm]), writes=[key])
    em.dma("Q", lambda e: e.dma_start(out=normw[:, :], in_=din["norm_w2"].rearrange("a p -> p a"), allow_slow_non_contiguous=True),
           writes=["normw"])
    em.op("v", lambda e: e.tensor_copy(identf[:, :], ident[:, :]), reads=["ident"], writes=["identf"])
    em.op("v", lambda e: e.memset(orow_f[:, :], 1.0), writes=["orow_f"])
    em.dma("Q", lambda e: e.dma_start(
        out=Pd["b_sel"][:, :].rearrange("h (s m) -> h s m", m=128),
        in_=din["sel"].rearrange("s h m -> h s m")), writes=["b_sel"])
    em.op("v", lambda e: e.memset(onesf[:, :], 1.0), writes=["onesf"])

    em.op("v", lambda e: e.memset(small[:, 0:1], 0.7), writes=["sm0"])
    em.op("s", lambda e: e.copy(ktmpi[:, :], small[:, 0:1]), reads=["sm0"], writes=["ktmpi"])
    em.op("s", lambda e: e.activation(kflag[:, :], ktmpi[:, :], AF.Copy, bias=256.5, scale=-0.5),
          reads=["ktmpi"], writes=["kflag"])

    a_x2 = A.at(0, [128, D], F32, "a_x2")
    a_ln = A.at(4096, [128, D], BF16, "a_ln")
    a_fm = A.at(6144, [128, 8 * 128], BF16, "a_fm")
    a_wgate = A.at(8192, [128, 8 * 4], BF16, "a_wgate")
    a_lgits = A.at(8320, [4, NCH_L * 128], F32, "a_lgits")
    a_resid = A.at(24 * 1024, [128, NCH_L * D], F32, "a_resid")

    em.dma("P", lambda e: e.dma_start(
        out=bass.AP(a_wgate, 0, [[32, 128], [4, 8], [1, 4]]),
        in_=din["wgate"].rearrange("(k p) m -> p k m", p=128)), writes=["wgate_s"])
    em.dma("Q", lambda e: e.dma_start(
        out=a_resid[:, :].rearrange("p (j d) -> p j d", d=D),
        in_=din["x_my"].rearrange("(j p) d -> p j d", p=128)),
        writes=["resid"])

    import os as _osL
    LNSTEP = int(_osL.environ.get("LNSTEP", "8"))

    def emit_ln(src_ap, src_keys, out_ap, out_key, scr=None, scrkey="a_x2"):
        st = LNSTEP
        if scr is None:
            scr = a_x2[:, :]
        if st >= 1:
            em.op("s", lambda e: e.activation(scr, src_ap, AF.Square,
                                              accum_out=small[:, 1:2]),
                  reads=src_keys, writes=[scrkey, "sm1"])
        else:
            em.op("v", lambda e: e.memset(small[:, 1:2], 1000.0), writes=["sm1"])
        if st >= 2:
            em.op("v", lambda e: e.reduce_sum(small[:, 2:3], src_ap, axis=AX.X),
                  reads=src_keys, writes=["sm2"])
        else:
            em.op("v", lambda e: e.memset(small[:, 2:3], 0.0), writes=["sm2"])
        if st >= 3:
            em.op("v", lambda e: e.tensor_single_scalar(small[:, 2:3], small[:, 2:3], 1.0 / D, ALU.mult),
                  reads=["sm2"], writes=["sm2"])
        if st >= 4:
            em.op("v", lambda e: e.tensor_tensor(small[:, 3:4], small[:, 2:3], small[:, 2:3], ALU.mult),
                  reads=["sm2"], writes=["sm3"])
            em.op("v", lambda e: e.scalar_tensor_tensor(small[:, 1:2], small[:, 1:2], 1.0 / D,
                                                        small[:, 3:4], ALU.mult, ALU.subtract),
                  reads=["sm1", "sm3"], writes=["sm1"])
        if st >= 5:
            em.op("s", lambda e: e.activation(small[:, 3:4], small[:, 1:2], AF.Ln, bias=1e-5),
                  reads=["sm1"], writes=["sm3"])
        else:
            em.op("v", lambda e: e.memset(small[:, 3:4], 1.0), writes=["sm3"])
        if st >= 6:
            em.op("s", lambda e: e.activation(small[:, 3:4], small[:, 3:4], AF.Exp, scale=-0.5),
                  reads=["sm3"], writes=["sm3"])
        oks = [out_key] if isinstance(out_key, str) else list(out_key)
        if st >= 7:
            em.op("v", lambda e: e.tensor_scalar(out_ap, src_ap, small[:, 2:3], small[:, 3:4],
                                                 ALU.subtract, ALU.mult),
                  reads=list(src_keys) + ["sm2", "sm3"], writes=oks)
        else:
            em.op("v", lambda e: e.memset(out_ap, 0.01), writes=oks)

    import os as _osA
    P0N = int(_osA.environ.get("P0N", str(NCH_L)))
    P0STEP = int(_osA.environ.get("P0STEP", "5"))
    for j in range(P0N):
        rsl = a_resid[:, j * D:(j + 1) * D]
        emit_ln(rsl, ["resid"], a_ln[:, :], "a_ln")
        if P0STEP < 2:
            continue
        em.mm_group(
            [lambda e, k=k: e.transpose(PB[0][:, k * 128:(k + 1) * 128],
                                        a_ln[:, k * 128:(k + 1) * 128], ident[:, :])
             for k in range(8)],
            reads=["a_ln", "ident"], writes=["pb0"])
        if P0STEP < 3:
            continue
        em.op("v", lambda e: e.tensor_copy(a_fm[:, :], PB[0][:, :]),
              reads=["pb0"], writes=["a_fm"])
        if P0STEP < 4:
            continue
        em.dma("Q", lambda e, j=j: e.dma_start(
            out=bass.AP(ag1_in.tensor, j * 8 * 128 * 128, [[128, 128], [128 * 128, 8], [1, 128]]),
            in_=a_fm[:, :].rearrange("p (k t) -> p k t", k=8)),
            reads=["a_fm"], writes=[f"ag1blk{j}"])
        if P0STEP < 5:
            continue
        em.mm_group(
            [lambda e, k=k: e.matmul(PS[5][0:4, 0:128], a_wgate[:, k * 4:(k + 1) * 4],
                                     a_fm[:, k * 128:(k + 1) * 128],
                                     start=(k == 0), stop=(k == 7)) for k in range(8)],
            reads=["wgate_s", "a_fm"], writes=["ps5"])
        em.op("v", lambda e, j=j: e.tensor_copy(a_lgits[:, j * 128:(j + 1) * 128], PS[5][0:4, 0:128]),
              reads=["ps5"], writes=[f"lg{j}"])

    em.dma("Q", lambda e: e.dma_start(out=ag2_in, in_=a_lgits[0:4, :]),
           reads=[f"lg{j}" for j in range(P0N if P0STEP >= 5 else 0)], writes=["ag2_in"])
    if P0N < NCH_L:
        pass
    if _osA.environ.get("SKIP_AG") != "1":
        em.cc(lambda e: e.collective_compute(
            "AllGather", ALU.bypass, replica_groups=GROUPS, ins=[ag1_in.opt()], outs=[ag1_out.opt()]),
            reads=[f"ag1blk{j}" for j in range(P0N)], writes=["ag1_out"])
    if _osA.environ.get("SKIP_AG") != "1":
        em.cc(lambda e: e.collective_compute(
            "AllGather", ALU.bypass, replica_groups=GROUPS, ins=[ag2_in.opt()], outs=[ag2_out.opt()]),
            reads=["ag2_in"], writes=["ag2_out"])

    import os as _os0
    SKIP_ROUTING = _os0.environ.get("SKIP_ROUTING") == "1"
    a_lgg = A.at(92 * 1024, [16, TPC], F32, "a_lgg")
    if SKIP_ROUTING:
        em.op("v", lambda e: e.memset(u_all[:, :], 0.001), writes=["u_all"])
        em.op("v", lambda e: e.memset(keep_all[:, :], 1.0), writes=["keep"])
    
    a_lgT = A.at(100 * 1024, [128, 256], F32, "a_lgT")
    a_r = A.at(101 * 1024, [128, 256], F32, "a_r")
    a_p1 = A.at(102 * 1024, [128, 256], F32, "a_p1")
    a_tot = A.at(103 * 1024, [64, 4], F32, "a_tot")
    a_thr = A.at(103 * 1024 + 32, [128, 4], F32, "a_thr")
    a_orow = A.at(103 * 1024 + 64, [1, 256], F32, "a_orow")

    if not SKIP_ROUTING:
        em.dma("Q", lambda e: e.dma_start(out=a_lgg[:, :], in_=ag2_out),
               reads=["ag2_out"], writes=["lgg"])

        def q3(ap):
            return ap.rearrange("p (q m) -> p q m", m=4)

        for jj in range(NCH_L):
            em.mm_group([lambda e, jj=jj: e.transpose(
                PS[4][:, 0:16], a_lgg[0:16, jj * 128:(jj + 1) * 128], identf[0:16, 0:16])],
                reads=["lgg", "identf"], writes=["ps4"])
            T3 = q3(PS[4][:, 0:16])
            lg = a_lgT[:, jj * 16:(jj + 1) * 16]
            em.op("v", lambda e, T3=T3: e.reduce_max(small[:, 4:8], T3, axis=AX.X),
                  reads=["ps4"], writes=["sm47"])
            m1b = small[:, 4:8].unsqueeze(2).broadcast_to([128, 4, 4])
            em.op("v", lambda e, T3=T3, lg=lg, m1b=m1b: e.tensor_tensor(q3(lg), T3, m1b, ALU.subtract),
                  reads=["ps4", "sm47"], writes=["lgT"])
            em.op("s", lambda e, lg=lg: e.activation(small[:, 8:24], lg, AF.Exp),
                  reads=["lgT"], writes=["sm8"])
            em.op("v", lambda e, lg=lg: e.tensor_single_scalar(small[:, 24:40], lg, 0.0, ALU.is_ge),
                  reads=["lgT"], writes=["sm24"])
            em.op("v", lambda e, lg=lg: e.scalar_tensor_tensor(small[:, 40:56], small[:, 24:40], -1e9,
                                                               lg, ALU.mult, ALU.add),
                  reads=["sm24", "lgT"], writes=["sm40"])
            em.op("v", lambda e: e.reduce_max(small[:, 56:60], q3(small[:, 40:56]), axis=AX.X),
                  reads=["sm40"], writes=["sm56"])
            m2b = small[:, 56:60].unsqueeze(2).broadcast_to([128, 4, 4])
            rr = a_r[:, jj * 16:(jj + 1) * 16]
            em.op("v", lambda e, lg=lg, rr=rr, m2b=m2b: e.tensor_tensor(q3(rr), q3(lg), m2b, ALU.is_ge),
                  reads=["lgT", "sm56"], writes=["r_all"])
            em.op("v", lambda e, rr=rr: e.tensor_tensor(small[:, 8:24], small[:, 8:24], rr, ALU.mult),
                  reads=["sm8", "r_all"], writes=["sm8"])
            em.op("v", lambda e: e.reduce_sum(small[:, 56:60], q3(small[:, 8:24]), axis=AX.X),
                  reads=["sm8"], writes=["sm56"])
            em.op("v", lambda e: e.reciprocal(small[:, 56:60], small[:, 56:60]),
                  reads=["sm56"], writes=["sm56"])
            seb = small[:, 56:60].unsqueeze(2).broadcast_to([128, 4, 4])
            em.op("v", lambda e, jj=jj, seb=seb: e.tensor_tensor(
                q3(u_all[:, jj * 16:(jj + 1) * 16]), q3(small[:, 8:24]), seb, ALU.mult),
                  reads=["sm8", "sm56"], writes=["u_all"])
            em.mm_group([lambda e, rr=rr: e.matmul(PS[3][:, 0:16], triI[:, :], rr, start=True, stop=True)],
                        reads=["triI", "r_all"], writes=["ps3"])
            em.op("v", lambda e, jj=jj: e.tensor_copy(a_p1[:, jj * 16:(jj + 1) * 16], PS[3][:, 0:16]),
                  reads=["ps3"], writes=["p1s"])

        em.dma("Q", lambda e: e.dma_start(out=bass.AP(rt_b.tensor, 0, [[256, 1], [1, 256]]),
                                          in_=a_p1[127:128, :]),
               reads=["p1s"], writes=["rtb"])
        em.dma("Q", lambda e: e.dma_start(
            out=a_tot[:, :],
            in_=bass.AP(rt_b.tensor, 0, [[4, 4], [16, 16], [1, 4]])),
            reads=["rtb"], writes=["tot"])
        em.mm_group([lambda e: e.matmul(PS[3][0:64, 0:4], triX[0:64, 0:64], a_tot[:, :],
                                        start=True, stop=True),
                     lambda e: e.matmul(PS[3][0:1, 8:12], onesf[0:64, :], a_tot[:, :],
                                        start=True, stop=True)],
                    reads=["triX", "tot", "onesf"], writes=["ps3"])
        em.op("v", lambda e: e.tensor_copy(small[0:64, 60:64], PS[3][0:64, 0:4]),
              reads=["ps3"], writes=["Osb"])
        em.op("v", lambda e: e.tensor_single_scalar(row1[0:1, 0:4], PS[3][0:1, 8:12], -4095.5, ALU.add),
              reads=["ps3"], writes=["thr"])
        em.mm_group([lambda e: e.matmul(PS[3][:, 16:20], orow_f[0:1, :], row1[0:1, 0:4],
                                        start=True, stop=True)],
                    reads=["orow_f", "thr", "Osb"], writes=["ps3"])
        em.op("v", lambda e: e.tensor_copy(a_thr[:, :], PS[3][:, 16:20]),
              reads=["ps3"], writes=["thrbc"])
        em.dma("Q", lambda e: e.dma_start(out=bass.AP(rt_b.tensor, 256, [[4, 64], [1, 4]]),
                                          in_=small[0:64, 60:64]),
               reads=["Osb"], writes=["rtb2"])
        em.dma("Q", lambda e: e.dma_start(
            out=a_orow[:, :],
            in_=bass.AP(rt_b.tensor, 256, [[256, 1], [4, 16], [64, 4], [1, 4]])),
            reads=["rtb2"], writes=["orow"])

        for jj in range(NCH_L):
            em.mm_group([lambda e, jj=jj: e.matmul(PS[3][:, 32:48], orow_f[0:1, :],
                                                   a_orow[0:1, jj * 16:(jj + 1) * 16],
                                                   start=True, stop=True)],
                        reads=["orow_f", "orow", "thrbc"], writes=["ps3"])
            em.op("v", lambda e, jj=jj: e.tensor_tensor(a_p1[:, jj * 16:(jj + 1) * 16],
                                                        a_p1[:, jj * 16:(jj + 1) * 16],
                                                        PS[3][:, 32:48], ALU.add),
                  reads=["p1s", "ps3"], writes=["p1s"])
            thr_b = a_thr[:, :].unsqueeze(1).broadcast_to([128, 4, 4])
            em.op("v", lambda e, jj=jj, thr_b=thr_b: e.tensor_tensor(
                q3(small[:, 8:24]), q3(a_p1[:, jj * 16:(jj + 1) * 16]), thr_b, ALU.is_ge),
                  reads=["p1s", "thrbc"], writes=["sm8"])
            em.op("v", lambda e, jj=jj: e.tensor_tensor(keep_all[:, jj * 16:(jj + 1) * 16],
                                                        a_r[:, jj * 16:(jj + 1) * 16],
                                                        small[:, 8:24], ALU.mult),
                  reads=["r_all", "sm8"], writes=["keep"])
            em.op("v", lambda e, jj=jj: e.scalar_tensor_tensor(u_all[:, jj * 16:(jj + 1) * 16],
                                                               u_all[:, jj * 16:(jj + 1) * 16], HDK_SC,
                                                               keep_all[:, jj * 16:(jj + 1) * 16],
                                                               ALU.mult, ALU.mult),
                  reads=["u_all", "keep"], writes=["u_all"])




    a_wq = A.at(16640, [128, 1024], F32, "a_wq")
    a_wqo = A.at(20736, [128, 1024], BF16, "a_wqo")
    a_qacc = A.at(22784, [128, 4], F32, "a_qacc")
    a_qsc = S_("a_qsc", [128, 8], F32)
    a_qrow = S_("a_qrow", [1, 8], F32)

    qsrc = [("g", din["fgT_my"], 4, 1024), ("u", din["fuT_my"], 4, 1024),
            ("d", din["fdT_my"], 4, 1024)]
    em.op("v", lambda e: e.memset(a_qacc[:, :], 0.0), writes=["qacc"])
    for wi, (tag, dsrc, ntile, tw) in enumerate(qsrc):
        for t_ in range(ntile):
            if tag == "d":
                sl = dsrc[t_ * 128:(t_ + 1) * 128, :]
            else:
                sl = dsrc[:, t_ * 1024:(t_ + 1) * 1024]
            em.dma("Q", lambda e, sl=sl: e.dma_start(out=a_wq[:, :], in_=sl), writes=["wq"])
            em.op("v", lambda e: e.reduce_sum(a_qsc[:, 7:8], a_wq[:, :], axis=AX.X,
                                              apply_absolute_value=True),
                  reads=["wq"], writes=["qtmp"])
            em.op("v", lambda e, wi=wi: e.tensor_tensor(a_qacc[:, wi:wi + 1], a_qacc[:, wi:wi + 1],
                                                        a_qsc[:, 7:8], ALU.add),
                  reads=["qacc", "qtmp"], writes=["qacc"])
    em.mm_group([lambda e, wi=wi: e.matmul(PS[3][0:1, wi:wi + 1], onesf[:, :],
                                           a_qacc[:, wi:wi + 1], start=True, stop=True)
                 for wi in range(3)],
                reads=["onesf", "qacc"], writes=["ps3"])
    em.op("v", lambda e: e.tensor_copy(a_qrow[0:1, 0:3], PS[3][0:1, 0:3]),
          reads=["ps3"], writes=["qrow"])
    em.op("v", lambda e: e.memset(a_qrow[0:1, 3:4], 0.0), reads=["qrow"], writes=["qrow"])
    em.dma("Q", lambda e: e.dma_start(out=asum_in, in_=a_qrow[0:1, 0:4]),
           reads=["qrow"], writes=["asum_in"])
    em.cc(lambda e: e.collective_compute(
        "AllReduce", ALU.add, replica_groups=[list(range(8))],
        ins=[asum_in.opt()], outs=[asum_out.opt()]),
        reads=["asum_in"], writes=["asum_out"])
    em.dma("Q", lambda e: e.dma_start(out=a_qrow[0:1, 0:4], in_=asum_out),
           reads=["asum_out"], writes=["qrow"])
    em.op("v", lambda e: e.tensor_scalar(a_qrow[0:1, 4:8], a_qrow[0:1, 0:4],
                                         1.0 / (D * INNER), 1e-5, ALU.mult, ALU.max),
          reads=["qrow"], writes=["qrow"])
    em.mm_group([lambda e: e.matmul(PS[3][:, 8:12], orow_f[0:1, :], a_qrow[0:1, 4:8],
                                    start=True, stop=True)],
                reads=["orow_f", "qrow"], writes=["ps3"])
    em.op("v", lambda e: e.tensor_copy(a_qsc[:, 4:8], PS[3][:, 8:12]),
          reads=["ps3"], writes=["qsc"])
    em.op("v", lambda e: e.reciprocal(a_qsc[:, 0:3], a_qsc[:, 4:7]),
          reads=["qsc"], writes=["qsc"])
    a_wqi = A.at(100 * 1024, [128, 1024], I32, "a_wqi")
    for wi, (tag, dsrc, ntile, tw) in enumerate(qsrc):
        tin = tern_in[tag]
        for t_ in range(ntile):
            if tag == "d":
                sl = dsrc[t_ * 128:(t_ + 1) * 128, :]
                dst = tin[t_ * 128:(t_ + 1) * 128, :]
            else:
                sl = dsrc[:, t_ * 1024:(t_ + 1) * 1024]
                dst = tin[:, t_ * 1024:(t_ + 1) * 1024]
            em.dma("Q", lambda e, sl=sl: e.dma_start(out=a_wq[:, :], in_=sl), writes=["wq"])
            em.op("s", lambda e, wi=wi: e.activation(a_wqi[:, :], a_wq[:, :], AF.Identity,
                                                     scale=a_qsc[:, wi:wi + 1], bias=kflag[:, 0:1]),
                  reads=["wq", "qsc", "kflag"], writes=["wqi", "lgT", "r_all", "p1s"])
            em.op("s", lambda e: e.activation(a_wq[:, :], a_wqi[:, :], AF.Copy, bias=-256.0),
                  reads=["wqi"], writes=["wq"])
            em.op("v", lambda e: e.tensor_scalar(a_wqo[:, :], a_wq[:, :], 1.0, -1.0,
                                                 ALU.min, ALU.max),
                  reads=["wq"], writes=["wqo"])
            em.dma("Q", lambda e, dst=dst: e.dma_start(out=dst, in_=a_wqo[:, :]),
                  reads=["wqo"], writes=["ternin"])
    for tag in ("g", "u", "d"):
        em.cc(lambda e, tag=tag: e.collective_compute(
            "AllGather", ALU.bypass, replica_groups=[list(range(8))],
            ins=[tern_in[tag].opt()], outs=[tern_out[tag].opt()]),
            reads=["ternin"], writes=[f"tern_{tag}"])
    em.dma("P", lambda e: e.dma_start(out=wqkvT_bf, in_=din["wqkvT"]), writes=["wqkvbf"])
    em.dma("P", lambda e: e.dma_start(out=wprojT_bf, in_=din["wprojT"]), writes=["wprojbf"])

    W2 = 1488
    aw_P2 = A.at(105 * 1024, [128, 8 * W2], BF16, "aw_P2")
    aw_k = A.at(129 * 1024, [128, 5 * 8 * 128], BF16, "aw_k")
    aw_g = A.at(139 * 1024, [128, 8 * 256], BF16, "aw_g")
    aw_wo = A.at(143 * 1024, [128, 2048], BF16, "aw_wo")
    aw_w2 = A.at(147 * 1024, [96, 5 * 128], BF16, "aw_w2")
    a_xg = [A.at(149 * 1024, [128, 8 * 512], BF16, "a_xg0"),
            A.at(157 * 1024, [128, 8 * 512], BF16, "a_xg1")]
    a_kfm = A.at(165 * 1024, [128, 5 * 512], BF16, "a_kfm")
    a_gsil = A.at(170 * 1024, [128, 2 * 512], BF16, "a_gsil")
    a_S = A.at(172 * 1024, [128, 5 * 256], BF16, "a_S")
    a_g1tpl = A.at(175 * 1024, [128, 160], BF16, "a_g1tpl")
    a_ush = A.at(175 * 1024 + 320, [128, 1], F32, "a_ush")
    a_qtm = A.at(0, [128, 128], BF16, "a_qtm")
    a_g1 = A.at(256, [128, 160], BF16, "a_g1")
    a_vtm = A.at(576, [128, 1280], BF16, "a_vtm")
    a_sc1 = A.at(3136, [128, 128], F32, "a_sc1")
    a_gkp = A.at(3648, [128, 128], BF16, "a_gkp")
    a_g1T = A.at(3904, [128, 256], BF16, "a_g1T")
    a_E = A.at(4416, [128, 128], BF16, "a_E")
    a_R = A.at(4672, [128, 128], BF16, "a_R")
    a_EL = A.at(4928, [128, 1], F32, "a_EL")
    a_qs = A.at(4960, [128, 128], BF16, "a_qs")
    a_Qt = A.at(5216, [128, 128], BF16, "a_Qt")
    a_Kt = A.at(5472, [128, 128], BF16, "a_Kt")
    a_Kh = A.at(5728, [128, 128], BF16, "a_Kh")
    a_KhT = A.at(5984, [128, 128], BF16, "a_KhT")
    a_A = A.at(6240, [128, 128], BF16, "a_A")
    a_osq = A.at(6496, [128, 256], F32, "a_osq")
    a_row = A.at(7520, [1, 128], F32, "a_row")
    a_rbc = A.at(8032, [128, 128], F32, "a_rbc")
    a_on = A.at(8544, [128, 256], BF16, "a_on")
    a_ofin = A.at(9056, [128, 256], BF16, "a_ofin")
    a_po = A.at(9568, [128, 1024], BF16, "a_po")
    a_rst = A.at(11616, [128, 1024], BF16, "a_rst")

    SKIP_WLOAD = _os0.environ.get("SKIP_WLOAD") == "1"
    if not SKIP_WLOAD:
        em.dma("P", lambda e: e.dma_start(
            out=bass.AP(aw_P2, 0, [[8 * W2, 128], [W2, 8], [1, 128]]),
            in_=din["wq_h"].rearrange("(k p) o -> p k o", p=128)), writes=["awP2q"])
        em.dma("P", lambda e: e.dma_start(
            out=bass.AP(aw_P2, 128, [[8 * W2, 128], [W2, 8], [1, 80]]),
            in_=din["wgk1"].rearrange("(k p) o -> p k o", p=128)), writes=["awP2g1"])
        for _m in range(4):
            em.dma("P", lambda e, _m=_m: e.dma_start(
                out=bass.AP(aw_P2, 208 + _m * 256, [[8 * W2, 128], [W2, 8], [1, 256]]),
                in_=din["wv_h"][_m].rearrange("(k p) o -> p k o", p=128)), writes=["awP2v"])
        em.dma("P", lambda e: e.dma_start(
            out=bass.AP(aw_P2, 208 + 1024, [[8 * W2, 128], [W2, 8], [1, 256]]),
            in_=din["wvs_h"].rearrange("(k p) o -> p k o", p=128)), writes=["awP2vs"])
        for _m in range(4):
            em.dma("P", lambda e, _m=_m: e.dma_start(
                out=bass.AP(aw_k, _m * 8 * 128, [[5 * 8 * 128, 128], [128, 8], [1, 128]]),
                in_=din["wk_h"][_m].rearrange("(k p) o -> p k o", p=128)), writes=["awk_m"])
        em.dma("P", lambda e: e.dma_start(
            out=bass.AP(aw_k, 4 * 8 * 128, [[5 * 8 * 128, 128], [128, 8], [1, 128]]),
            in_=din["wks_h"].rearrange("(k p) o -> p k o", p=128)), writes=["awk_s"])
        em.dma("P", lambda e: e.dma_start(
            out=bass.AP(aw_g, 0, [[8 * 256, 128], [256, 8], [1, 256]]),
            in_=din["wg_h"].rearrange("(k p) o -> p k o", p=128)), writes=["awg"])
        for _vt in range(2):
            em.dma("P", lambda e, _vt=_vt: e.dma_start(
                out=bass.AP(aw_wo, _vt * 8 * 128, [[2048, 128], [128, 8], [1, 128]]),
                in_=din["wo_h"][_vt * 128:(_vt + 1) * 128].rearrange("p (k o) -> p k o", o=128)),
                writes=["awwo"])
        for _p in range(5):
            _sp = _p if _p < 3 else _p - 3
            em.dma("P", lambda e, _p=_p, _sp=_sp: e.dma_start(
                out=aw_w2[32 * _sp:32 * _sp + 17, _p * 128:(_p + 1) * 128],
                in_=din["w2aug"][_p]), writes=["aww2"])


    em.op("v", lambda e: e.memset(a_g1tpl[:, :], 0.0), writes=["g1tpl"])
    for i in range(5):
        em.op("v", lambda e, i=i: e.memset(a_g1tpl[:, 32 * i + 16:32 * i + 17], 1.0),
              reads=["g1tpl"], writes=["g1tpl"])
    em.op("v", lambda e: e.memset(a_ush[:, :], HDK_SC), writes=["ush"])
    em.op("v", lambda e: e.memset(a_S[:, :], 0.0), writes=["aS"])

    import os as _os
    NGRP = int(_os.environ.get("ATTNA_GROUPS", "16"))
    NPATH = 5
    for G in range(NGRP):
        q, g_in_rank = G // 4, G % 4
        jj0 = g_in_rank * 4
        xg = a_xg[G % 2]
        xgk = f"xg{G % 2}"
        for k in range(8):
            base = ((q * 16 + jj0) * 8 + k) * (128 * 128)
            em.dma("Q", lambda e, base=base, k=k, xg=xg: e.dma_start(
                out=bass.AP(xg, k * 512, [[8 * 512, 128], [128, 4], [1, 128]]),
                in_=bass.AP(ag1_out.tensor, base, [[128, 128], [8 * 128 * 128, 4], [1, 128]])),
                reads=["ag1_out"], writes=[xgk])
        for path in range(NPATH):
            em.mm_group(
                [lambda e, path=path, k=k, xg=xg: e.matmul(
                    PS[0][:, :], aw_k[:, (path * 8 + k) * 128:(path * 8 + k + 1) * 128],
                    xg[:, k * 512:(k + 1) * 512], start=(k == 0), stop=(k == 7))
                 for k in range(8)],
                reads=["awk_m", "awk_s", xgk], writes=["ps0"])
            em.op("v" if path % 2 == 0 else "s",
                  lambda e, path=path: CP(e, a_kfm[:, path * 512:(path + 1) * 512], PS[0][:, :]),
                  reads=["ps0"], writes=["kfm"])
        for ot in range(2):
            em.mm_group(
                [lambda e, ot=ot, k=k, xg=xg: e.matmul(
                    PS[1][:, :], aw_g[:, k * 256 + ot * 128:k * 256 + (ot + 1) * 128],
                    xg[:, k * 512:(k + 1) * 512], start=(k == 0), stop=(k == 7))
                 for k in range(8)],
                reads=["awg", xgk], writes=["ps1"])
            em.op("s", lambda e, ot=ot: e.activation(a_gsil[:, ot * 512:(ot + 1) * 512],
                                                     PS[1][:, :], AF.Silu),
                  reads=["ps1"], writes=["gsil"])

        for jj in range(4):
            jl = jj0 + jj
            cj = q * 16 + jl
            tsl = slice(jj * 128, (jj + 1) * 128)
            fns = []
            for k in range(8):
                xs = xg[:, k * 512 + jj * 128:k * 512 + (jj + 1) * 128]
                w = lambda lo, hi, k=k: aw_P2[:, k * W2 + lo:k * W2 + hi]
                fns.append(lambda e, xs=xs, w=w, k=k: e.matmul(
                    PS[2][:, 256:464], xs, w(0, 208), start=(k == 0), stop=(k == 7), skip_group_check=True))
                fns.append(lambda e, xs=xs, w=w, k=k: e.matmul(
                    PS[0][:, :], xs, w(208, 720), start=(k == 0), stop=(k == 7), skip_group_check=True))
                fns.append(lambda e, xs=xs, w=w, k=k: e.matmul(
                    PS[1][:, :], xs, w(720, 1232), start=(k == 0), stop=(k == 7), skip_group_check=True))
                fns.append(lambda e, xs=xs, w=w, k=k: e.matmul(
                    PS[2][:, 0:256], xs, w(1232, 1488), start=False, stop=(k == 7), skip_group_check=True))
            em.mm_group(fns, reads=[xgk, "awP2q", "awP2g1", "awP2v", "awP2vs"],
                        writes=["ps0", "ps1", "ps2"])
            em.op("v", lambda e: e.tensor_copy(a_qtm[:, :], PS[2][:, 256:384]),
                  reads=["ps2"], writes=["qtm"])
            em.op("v", lambda e: e.tensor_copy(a_g1[:, :], a_g1tpl[:, :]),
                  reads=["g1tpl"], writes=["g1"])
            em.op("v", lambda e: e.tensor_copy(
                bass.AP(a_g1, 0, [[160, 128], [32, 5], [1, 16]]),
                PS[2][:, 384:464].rearrange("p (m o) -> p m o", m=5)),
                  reads=["ps2", "g1"], writes=["g1"])
            for m in range(4):
                kc = keep_all[:, jl * 16 + q * 4 + m:jl * 16 + q * 4 + m + 1]
                ps = PS[0] if m < 2 else PS[1]
                em.op("s", lambda e, m=m, ps=ps, kc=kc: e.activation(
                    a_vtm[:, m * 256:(m + 1) * 256], ps[:, (m % 2) * 256:(m % 2 + 1) * 256],
                    AF.Copy, scale=kc),
                      reads=["ps0" if m < 2 else "ps1", "keep"], writes=["vtm"])
            em.op("v", lambda e: e.tensor_copy(a_vtm[:, 1024:1280], PS[2][:, 0:256]),
                  reads=["ps2"], writes=["vtm"])
            em.mm_group(
                [lambda e: e.transpose(PB[0][0:96, 0:128], a_g1[:, 0:96], ident[:, :]),
                 lambda e: e.transpose(PB[0][0:64, 128:256], a_g1[:, 96:160], ident[:, :])],
                reads=["g1", "ident"], writes=["pb0"])
            em.op("v", lambda e: e.tensor_copy(a_g1T[0:96, 0:128], PB[0][0:96, 0:128]),
                  reads=["pb0"], writes=["g1T"])
            em.op("v", lambda e: e.tensor_copy(a_g1T[0:64, 128:256], PB[0][0:64, 128:256]),
                  reads=["pb0"], writes=["g1T"])

            first_o = True
            for path in range(NPATH):
                ucol = (u_all[:, jl * 16 + q * 4 + path:jl * 16 + q * 4 + path + 1]
                        if path < 4 else a_ush[:, 0:1])
                ukey = "u_all" if path < 4 else "ush"
                if path < 3:
                    g1s = a_g1T[32 * path:32 * path + 17, 0:128]
                else:
                    g1s = a_g1T[32 * (path - 3):32 * (path - 3) + 17, 128:256]
                sp = path if path < 3 else path - 3
                em.mm_group([lambda e, g1s=g1s, path=path, sp=sp: e.matmul(
                    PS[3][:, 0:128], g1s, aw_w2[32 * sp:32 * sp + 17, path * 128:(path + 1) * 128],
                    start=True, stop=True)],
                    reads=["g1T", "aww2"], writes=["ps3"])
                em.op("s", lambda e: e.activation(a_sc1[:, :], PS[3][:, 0:128], AF.Exp, scale=-1.0),
                      reads=["ps3"], writes=["sc1"])
                if path < 4:
                    em.op("s", lambda e: e.activation(a_sc1[:, :], a_sc1[:, :], AF.Ln, bias=1.0),
                          reads=["sc1"], writes=["sc1"])
                    kc = keep_all[:, jl * 16 + q * 4 + path:jl * 16 + q * 4 + path + 1]
                    em.op("v", lambda e, kc=kc: e.tensor_scalar(a_gkp[:, :], a_sc1[:, :], kc, None, ALU.mult),
                          reads=["sc1", "keep"], writes=["gkp"])
                else:
                    em.op("s", lambda e: e.activation(a_gkp[:, :], a_sc1[:, :], AF.Ln, bias=1.0),
                          reads=["sc1"], writes=["gkp"])
                em.mm_group([lambda e: e.matmul(PS[4][:, 0:128], a_gkp[:, :], lgk[:, :],
                                                start=True, stop=True)],
                            reads=["gkp", "lgk"], writes=["ps4"])
                em.op("s", lambda e: e.activation(a_E[:, :], PS[4][:, 0:128], AF.Exp),
                      reads=["ps4"], writes=["aE"])
                em.op("s", lambda e: e.activation(a_R[:, :], PS[4][:, 0:128], AF.Exp, scale=-1.0),
                      reads=["ps4"], writes=["aR"])
                em.op("s", lambda e: e.activation(a_EL[:, :], PS[4][:, 127:128], AF.Exp),
                      reads=["ps4"], writes=["aEL"])
                em.op("s", lambda e, ucol=ucol: e.activation(a_qs[:, :], a_qtm[:, :], AF.Copy, scale=ucol),
                      reads=["qtm", ukey], writes=["qs"])
                em.mm_group([lambda e: e.transpose(PB[1][:, 0:128], a_qs[:, :], ident[:, :])],
                            reads=["qs", "ident"], writes=["pb1"])
                em.op("v", lambda e: e.tensor_tensor(a_Qt[:, :], PB[1][:, 0:128], a_E[:, :], ALU.mult),
                      reads=["pb1", "aE"], writes=["Qt"])
                ksl = a_kfm[:, path * 512 + tsl.start:path * 512 + tsl.stop]
                em.op("v", lambda e, ksl=ksl: e.tensor_tensor(a_Kt[:, :], ksl, a_R[:, :], ALU.mult),
                      reads=["kfm", "aR"], writes=["Kt"])
                em.op("v", lambda e: e.tensor_scalar(a_Kh[:, :], a_Kt[:, :], a_EL[:, 0:1], None, ALU.mult),
                      reads=["Kt", "aEL"], writes=["Kh"])
                em.mm_group([lambda e: e.transpose(PB[1][:, 128:256], a_Kh[:, :], ident[:, :])],
                            reads=["Kh", "ident"], writes=["pb1"])
                em.op("v", lambda e: e.tensor_copy(a_KhT[:, :], PB[1][:, 128:256]),
                      reads=["pb1"], writes=["KhT"])
                em.mm_group([lambda e: e.matmul(PS[3][:, 128:256], a_Kt[:, :], a_Qt[:, :],
                                                start=True, stop=True)],
                            reads=["Kt", "Qt"], writes=["ps3"])
                em.op("v", lambda e: e.tensor_tensor(a_A[:, :], PS[3][:, 128:256], tri[:, :], ALU.mult),
                      reads=["ps3", "tri"], writes=["aA"])
                vt0 = a_vtm[:, path * 256:path * 256 + 128]
                vt1 = a_vtm[:, path * 256 + 128:path * 256 + 256]
                s0 = a_S[:, path * 256:path * 256 + 128]
                s1 = a_S[:, path * 256 + 128:path * 256 + 256]
                st = first_o
                em.mm_group(
                    [lambda e, vt0=vt0, st=st: e.matmul(PS[5][:, 0:128], vt0, a_A[:, :],
                                                        start=st, stop=False, skip_group_check=True),
                     lambda e, vt1=vt1: e.matmul(PS[5][:, 128:256], vt1, a_A[:, :],
                                                 start=False, stop=False, skip_group_check=True),
                     lambda e, s0=s0, path=path: e.matmul(PS[5][:, 0:128], s0, a_Qt[:, :],
                                               start=False, stop=(path == 4), skip_group_check=True),
                     lambda e, s1=s1, path=path: e.matmul(PS[5][:, 128:256], s1, a_Qt[:, :],
                                               start=False, stop=(path == 4), skip_group_check=True)],
                    reads=["vtm", "aA", "aS", "Qt"], writes=["ps5"])
                first_o = False
                em.mm_group([lambda e, path=path: e.matmul(
                    PS[4][:, 256:512], a_KhT[:, :], a_vtm[:, path * 256:(path + 1) * 256],
                    start=True, stop=True)],
                    reads=["KhT", "vtm"], writes=["ps4"])
                em.op("v", lambda e, path=path: e.scalar_tensor_tensor(
                    a_S[:, path * 256:(path + 1) * 256], a_S[:, path * 256:(path + 1) * 256],
                    a_EL[:, 0:1], PS[4][:, 256:512], ALU.mult, ALU.add),
                      reads=["aS", "aEL", "ps4"], writes=["aS"])

            em.op("s", lambda e: e.activation(a_osq[:, :], PS[5][:, 0:256], AF.Square),
                  reads=["ps5"], writes=["osq"])
            em.mm_group(
                [lambda e: e.matmul(PS[3][0:1, 256:384], onesf[:, :], a_osq[:, 0:128],
                                    start=True, stop=False),
                 lambda e: e.matmul(PS[3][0:1, 256:384], onesf[:, :], a_osq[:, 128:256],
                                    start=False, stop=True)],
                reads=["onesf", "osq"], writes=["ps3"])
            em.op("s", lambda e: e.activation(a_row[:, :], PS[3][0:1, 256:384], AF.Ln,
                                              scale=1.0 / 256, bias=1e-5),
                  reads=["ps3"], writes=["arow"])
            em.op("s", lambda e: e.activation(a_row[:, :], a_row[:, :], AF.Exp, scale=-0.5),
                  reads=["arow"], writes=["arow"])
            em.mm_group([lambda e: e.matmul(PS[3][:, 384:512], orow_f[0:1, :], a_row[:, :],
                                            start=True, stop=True)],
                        reads=["orow_f", "arow"], writes=["ps3"])
            em.op("v", lambda e: e.tensor_copy(a_rbc[:, :], PS[3][:, 384:512]),
                  reads=["ps3"], writes=["rbc"])
            for vt in range(2):
                em.op("v", lambda e, vt=vt: e.tensor_tensor(
                    a_on[:, vt * 128:(vt + 1) * 128], PS[5][:, vt * 128:(vt + 1) * 128],
                    a_rbc[:, :], ALU.mult),
                      reads=["ps5", "rbc", "osq"], writes=["aon"])
                gsl = a_gsil[:, vt * 512 + jj * 128:vt * 512 + (jj + 1) * 128]
                em.op("v", lambda e, vt=vt, gsl=gsl: e.scalar_tensor_tensor(
                    a_ofin[:, vt * 128:(vt + 1) * 128], a_on[:, vt * 128:(vt + 1) * 128],
                    normw[:, vt:vt + 1], gsl, ALU.mult, ALU.mult),
                      reads=["aon", "normw", "gsil"], writes=["ofin"])
            fns = []
            for ot in range(8):
                ps = PS[0] if ot < 4 else PS[1]
                col = (ot % 4) * 128
                for vt in range(2):
                    fns.append(lambda e, ot=ot, vt=vt, ps=ps, col=col: e.matmul(
                        ps[:, col:col + 128], aw_wo[:, (vt * 8 + ot) * 128:(vt * 8 + ot + 1) * 128],
                        a_ofin[:, vt * 128:(vt + 1) * 128],
                        start=(vt == 0 and (ot % 4) == 0), stop=(vt == 1), skip_group_check=True))
            em.mm_group(fns, reads=["awwo", "ofin"], writes=["ps0", "ps1"])
            em.op("v", lambda e: e.tensor_copy(a_po[:, 0:512], PS[0][:, :]),
                  reads=["ps0"], writes=["apo"])
            em.op("s", lambda e: e.copy(a_po[:, 512:1024], PS[1][:, :]),
                  reads=["ps1"], writes=["apo"])
            base = ((q * 16 + jl) * 8) * (128 * 128)
            em.dma("Q", lambda e, base=base: e.dma_start(
                out=bass.AP(rs_in.tensor, base, [[128, 128], [128 * 128, 8], [1, 128]]),
                in_=a_po[:, :].rearrange("p (k t) -> p k t", k=8)),
                reads=["apo"], writes=["rsin"])

    _do_rs = NGRP == 16 and _os.environ.get("SKIP_RS") != "1"
    (lambda: None)() if _do_rs else None
    if _do_rs:
        em.cc(lambda e: e.collective_compute(
            "ReduceScatter", ALU.add, replica_groups=GROUPS, ins=[rs_in.opt()], outs=[rs_out.opt()]),
            reads=["rsin"], writes=["rs_out"])
    for jl in range(NCH_L if _do_rs else 0):
        em.dma("Q", lambda e, jl=jl: e.dma_start(
            out=a_rst[:, :].rearrange("p (k t) -> p k t", k=8),
            in_=bass.AP(rs_out.tensor, jl * 8 * 128 * 128, [[128, 128], [128 * 128, 8], [1, 128]])),
            reads=["rs_out"], writes=["rst"])
        em.mm_group(
            [lambda e, k=k: e.transpose(PB[0][:, k * 128:(k + 1) * 128],
                                        a_rst[:, k * 128:(k + 1) * 128], ident[:, :])
             for k in range(8)],
            reads=["rst", "ident"], writes=["pb0"])
        for k in range(8):
            em.op("v", lambda e, jl=jl, k=k: e.tensor_tensor(
                a_resid[:, jl * D + k * 128:jl * D + (k + 1) * 128],
                a_resid[:, jl * D + k * 128:jl * D + (k + 1) * 128],
                PB[0][:, k * 128:(k + 1) * 128], ALU.add),
                  reads=["pb0", "resid"], writes=["resid"])

    if "d_res1" in dbg_out:
        em.dma("Q", lambda e: e.dma_start(out=dbg_out["d_res1"], in_=a_resid[:, :]),
               reads=["resid"], writes=["dres1"])



    ZONE_HI = ["lgg", "wqi", "lgT", "r_all", "p1s", "tot", "thrbc", "orow",
               "awP2q", "awP2g1", "awP2v", "awP2vs", "awk_m", "awk_s", "awg",
               "awwo", "aww2", "xg0", "xg1", "kfm"]
    f_tg = A.at(24 * 1024, [128, 8 * INNER], BF16, "f_tg")
    f_tu = A.at(88 * 1024, [128, 8 * INNER], BF16, "f_tu")
    f_fd = A.at(88 * 1024, [128, 32 * 1024], BF16, "f_fd")
    f_xq8 = A.at(152 * 1024, [128, NCH_L * 1024], I8, "f_xq8")
    f_xn = A.at(4096, [128, 1024], F32, "f_xn")
    f_xqb = A.at(8192, [128, 1024], BF16, "f_xqb")
    f_xi = A.at(10240, [128, 1024], I32, "f_xi")
    f_u = A.at(14336, [128, INNER], BF16, "f_u")
    f_sil = A.at(22528, [128, 512], BF16, "f_sil")
    f_res = A.at(0, [128, 1024], F32, "f_res")
    f_dn = A.at(4096, [128, 1024], BF16, "f_dn")
    f_ufm = A.at(6144, [128, 32 * 128], BF16, "f_ufm")
    f_uq2 = A.at(14336, [128, INNER], BF16, "f_uq2")

    ATTNB_ZA = ["b_xfm", "b_scr1", "b_kp", "b_vaug", "b_osc", "b_prj",
                "b_kvs", "b_kvb", "b_ksm8"]

    ZONE_A_ALL = (["a_ln", "a_x2", "wgate_s", "wq", "wqo", "qacc", "qsc", "qrow",
                   "qtm", "g1", "vtm", "sc1", "gkp", "g1T", "aE", "aR", "aEL", "qs",
                   "Qt", "Kt", "Kh", "KhT", "aA", "osq", "arow", "rbc", "aon",
                   "ofin", "apo", "rst", "f_res", "f_dn", "f_ufm", "f_uq2"]
                  + ["a_fm"]
                  + [f"lg{j}" for j in range(NCH_L)])

    def emit_ffn(idx):
        em.op("v", lambda e: e.memset(f_xn[:, :], 0.0),
              writes=["f_xn", "f_xqb", "f_xi", "f_u", "f_sil"] + ZONE_A_ALL
              + (ATTNB_ZA if idx == 1 else []))
        for j in range(NCH_L):
            rsl = a_resid[:, j * D:(j + 1) * D]
            emit_ln(rsl, ["resid"], f_xn[:, :], "f_xn")
            em.op("v", lambda e: e.reduce_max(small[:, 8:9], f_xn[:, :], axis=AX.X,
                                              apply_absolute_value=True),
                  reads=["f_xn"], writes=["sm8"])
            em.op("v", lambda e: e.tensor_single_scalar(small[:, 8:9], small[:, 8:9], 1e-5, ALU.max),
                  reads=["sm8"], writes=["sm8"])
            em.op("v", lambda e, j=j: e.tensor_single_scalar(invsx[:, j:j + 1], small[:, 8:9],
                                                             1.0 / 127, ALU.mult),
                  reads=["sm8"], writes=["invsx"])
            em.op("v", lambda e: e.reciprocal(small[:, 9:10], small[:, 8:9]),
                  reads=["sm8"], writes=["sm9"])
            em.op("v", lambda e: e.tensor_single_scalar(small[:, 9:10], small[:, 9:10], 127.0, ALU.mult),
                  reads=["sm9"], writes=["sm9"])
            em.op("s", lambda e: e.activation(f_xi[:, :], f_xn[:, :], AF.Identity,
                                              scale=small[:, 9:10], bias=kflag[:, 0:1]),
                  reads=["f_xn", "sm9", "kflag"], writes=["f_xi"])
            em.op("s", lambda e: e.activation(f_xqb[:, :], f_xi[:, :], AF.Copy, bias=-256.0),
                  reads=["f_xi"], writes=["f_xqb"])
            em.mm_group(
                [lambda e, k=k: e.transpose(PB[0][:, k * 128:(k + 1) * 128],
                                            f_xqb[:, k * 128:(k + 1) * 128], ident[:, :])
                 for k in range(8)],
                reads=["f_xqb", "ident"], writes=["pb0"])
            em.op("v", lambda e, j=j: e.tensor_copy(f_xq8[:, j * 1024:(j + 1) * 1024], PB[0][:, :]),
                  reads=["pb0"],
                  writes=(["f_xq8"] + (["xg0", "xg1", "kfm"] if j == 0 and idx == 0 else [])
                          + (["b_qp"] if j == 0 and idx == 1 else [])))
        em.dma("Q", lambda e: e.dma_start(out=resid_park, in_=a_resid[:, :]),
               reads=["resid"], writes=["parked"])
        for k in range(8):
            em.dma("Q", lambda e, k=k: e.dma_start(
                out=f_tg[:, k * INNER:(k + 1) * INNER],
                in_=tern_out["g"][k * 128:(k + 1) * 128, :]),
                reads=["tern_g", "parked"], writes=["resid"])
            em.dma("Q", lambda e, k=k: e.dma_start(
                out=f_tu[:, k * INNER:(k + 1) * INNER],
                in_=tern_out["u"][k * 128:(k + 1) * 128, :]),
                reads=["tern_u"], writes=["f_tu"] + (ZONE_HI if k == 0 and idx == 0 else [])
                + (["b_wqkv", "b_qp"] if k == 0 and idx == 1 else []))
        SETS = [list(range(0, 3)), list(range(3, 6)), list(range(6, 8))]
        for j in range(NCH_L):
            em.dma("P", lambda e, j=j: e.dma_start(
                out=f_xqb[:, :], in_=f_xq8[:, j * 1024:(j + 1) * 1024]),
                reads=["f_xq8"], writes=["f_xqb"])
            em.op("v", lambda e, j=j: e.tensor_tensor(small[:, 10:11], invsx[:, j:j + 1],
                                                      a_qsc[:, 4:5], ALU.mult),
                  reads=["invsx", "qsc"], writes=["sm10"])
            em.op("v", lambda e, j=j: e.tensor_tensor(small[:, 11:12], invsx[:, j:j + 1],
                                                      a_qsc[:, 5:6], ALU.mult),
                  reads=["invsx", "qsc"], writes=["sm11"])
            for si, spans in enumerate(SETS):
                fns = []
                for k in range(8):
                    xs = f_xqb[:, k * 128:(k + 1) * 128]
                    for bi, sp in enumerate(spans):
                        for w_i, wt in enumerate((f_tg, f_tu)):
                            bank = PS[bi * 2 + w_i]
                            fns.append(lambda e, xs=xs, wt=wt, sp=sp, bank=bank, k=k: e.matmul(
                                bank[:, :], xs, wt[:, k * INNER + sp * 512:k * INNER + (sp + 1) * 512],
                                start=(k == 0), stop=(k == 7), skip_group_check=True))
                em.mm_group(fns, reads=["f_xqb", "resid", "f_tu", "sm10", "sm11"],
                            writes=[f"ps{b}" for b in range(2 * len(spans))])
                for bi, sp in enumerate(spans):
                    em.op("s", lambda e, bi=bi: e.activation(
                        f_sil[:, :], PS[bi * 2][:, :], AF.Silu, scale=small[:, 10:11]),
                          reads=[f"ps{bi * 2}", "sm10"], writes=["f_sil"])
                    em.op("v", lambda e, bi=bi, sp=sp: e.scalar_tensor_tensor(
                        f_u[:, sp * 512:(sp + 1) * 512], PS[bi * 2 + 1][:, :],
                        small[:, 11:12], f_sil[:, :], ALU.mult, ALU.mult),
                          reads=[f"ps{bi * 2 + 1}", "sm11", "f_sil"], writes=["f_u"])
            em.op("v", lambda e: e.reduce_max(small[:, 12:13], f_u[:, :], axis=AX.X,
                                              apply_absolute_value=True),
                  reads=["f_u"], writes=["sm12"])
            em.op("v", lambda e: e.tensor_single_scalar(small[:, 12:13], small[:, 12:13], 1e-5, ALU.max),
                  reads=["sm12"], writes=["sm12"])
            em.op("v", lambda e, j=j: e.scalar_tensor_tensor(
                small[:, 16 + j:17 + j], small[:, 12:13], 1.0 / 127, a_qsc[:, 6:7],
                ALU.mult, ALU.mult),
                  reads=["sm12", "qsc"], writes=[f"invsu{j}"])
            em.op("v", lambda e: e.reciprocal(small[:, 14:15], small[:, 12:13]),
                  reads=["sm12"], writes=["sm14"])
            em.op("v", lambda e: e.tensor_single_scalar(small[:, 14:15], small[:, 14:15], 127.0, ALU.mult),
                  reads=["sm14"], writes=["sm14"])
            for ct in range(4):
                em.op("s", lambda e, ct=ct: e.activation(
                    f_xi[:, :], f_u[:, ct * 1024:(ct + 1) * 1024], AF.Identity,
                    scale=small[:, 14:15], bias=kflag[:, 0:1]),
                      reads=["f_u", "sm14", "kflag"], writes=["f_xi"])
                em.op("s", lambda e: e.activation(f_xqb[:, :], f_xi[:, :], AF.Copy, bias=-256.0),
                      reads=["f_xi"], writes=["f_xqb"])
                em.dma("P", lambda e, j=j, ct=ct: e.dma_start(
                    out=bass.AP(uq_park.tensor, j * INNER + ct * 1024,
                                [[NCH_L * INNER, 128], [1, 1024]]),
                    in_=f_xqb[:, :]),
                    reads=["f_xqb"], writes=["uqpark"])
        for ik in range(4):
            em.dma("Q", lambda e, ik=ik: e.dma_start(
                out=f_fd[:, ik * 8 * 1024:(ik + 1) * 8 * 1024],
                in_=tern_out["d"][ik * 8 * 128:(ik + 1) * 8 * 128, :].rearrange(
                    "(k p) o -> p k o", p=128)),
                reads=["tern_d"], writes=["f_fd", "f_tu"])
        for j in range(NCH_L):
            em.dma("P", lambda e, j=j: e.dma_start(
                out=f_uq2[:, :],
                in_=bass.AP(uq_park.tensor, j * INNER, [[NCH_L * INNER, 128], [1, INNER]])),
                reads=["uqpark"], writes=["f_uq2", "f_u"])
            for tb in range(4):
                em.mm_group(
                    [lambda e, tb=tb, k=k: e.transpose(
                        PB[0][:, k * 128:(k + 1) * 128],
                        f_uq2[:, (tb * 8 + k) * 128:(tb * 8 + k + 1) * 128], ident[:, :])
                     for k in range(8)],
                    reads=["f_uq2", "ident"], writes=["pb0"])
                em.op("v", lambda e, tb=tb: e.tensor_copy(
                    f_ufm[:, tb * 1024:(tb + 1) * 1024], PB[0][:, :]),
                      reads=["pb0"],
                      writes=["f_ufm"] + (["f_xqb", "f_xi"] if tb == 0 and j == 0 else []))
            fns = []
            for ik in range(32):
                for osp in range(2):
                    fns.append(lambda e, ik=ik, osp=osp: e.matmul(
                        PS[osp][:, :], f_ufm[:, ik * 128:(ik + 1) * 128],
                        f_fd[:, ik * 1024 + osp * 512:ik * 1024 + (osp + 1) * 512],
                        start=(ik == 0), stop=(ik == 31), skip_group_check=True))
            em.mm_group(fns, reads=["f_ufm", "f_fd"], writes=["ps0", "ps1"])
            em.dma("Q", lambda e, j=j: e.dma_start(
                out=f_res[:, :],
                in_=bass.AP(resid_park.tensor, j * D, [[NCH_L * D, 128], [1, D]])),
                reads=["parked"], writes=["f_res"] + (["a_x2"] if j == 0 else []))
            for osp in range(2):
                em.op("s", lambda e, osp=osp, j=j: e.activation(
                    f_dn[:, osp * 512:(osp + 1) * 512], PS[osp][:, :], AF.Copy,
                    scale=small[:, 16 + j:17 + j]),
                      reads=[f"ps{osp}", f"invsu{j}"], writes=["f_dn"] + (["f_xn"] if osp == 0 and j == 0 else []))
            em.op("v", lambda e, j=j: e.tensor_tensor(
                a_resid[:, j * D:(j + 1) * D], f_res[:, :], f_dn[:, :], ALU.add),
                  reads=["f_res", "f_dn", "resid"], writes=["resid"])

    emit_ffn(0)
    if "d_res2" in dbg_out:
        em.dma("Q", lambda e: e.dma_start(out=dbg_out["d_res2"], in_=a_resid[:, :]),
               reads=["resid"], writes=["dres2"])


    b_wqkv = A.at(88 * 1024, [128, 8 * 3072], BF16, "b_wqkv")
    b_qp = A.at(136 * 1024, [128, 8 * 2048], BF16, "b_qp")
    b_qkv = A.at(168 * 1024, [128, 24 * 256], BF16, "b_qkv")
    b_xfm = A.at(0, [128, 8 * 256], BF16, "b_xfm")
    b_scr1 = A.at(4096, [128, 1024], F32, "b_scr1")
    b_kp = A.at(8192, [128, 1024], BF16, "b_kp")
    b_vaug = A.at(10240, [128, 1040], BF16, "b_vaug")
    b_osc = A.at(12320, [128, 2048], BF16, "b_osc")
    b_prj = A.at(16416, [128, 2048], BF16, "b_prj")
    b_kvs = A.at(20512, [128, 520], F32, "b_kvs")
    b_kvb = A.at(22592, [128, 520], BF16, "b_kvb")
    b_kvb2 = A.at(20512, [128, 1024], BF16, "b_kvb2")
    b_ksm8 = A.at(23648, [128, 128], BF16, "b_ksm8")
    b_rn = A.at(23904, [16, 256], BF16, "b_rn")

    def emit_attnb():
        em.op("v", lambda e: e.memset(b_xfm[:, :], 0.0),
              writes=ATTNB_ZA + ["f_xn", "f_xqb", "f_xi", "f_u", "f_sil"] + ZONE_A_ALL)
        em.op("v", lambda e: e.memset(Pd["b_pm2"][:, :], 0.0), writes=["b_pm2"])
        em.dma("P", lambda e: e.dma_start(out=Pd["b_pm2"][0:64, 0:64], in_=din["pmT"]),
               reads=["b_pm2"], writes=["b_pm2"])
        em.dma("P", lambda e: e.dma_start(out=Pd["b_pm2"][64:128, 64:128], in_=din["pmT"]),
               reads=["b_pm2"], writes=["b_pm2"])
        em.op("v", lambda e: e.tensor_single_scalar(Pd["b_pm2"][:, :], Pd["b_pm2"][:, :],
                                                    64.0 ** -0.25, ALU.mult),
              reads=["b_pm2"], writes=["b_pm2"])
        em.dma("Q", lambda e: e.dma_start(
            out=Pd["b_bq"][:, :], in_=din["bqkv"].rearrange("a (o p) -> p (a o)", p=128),
            allow_slow_non_contiguous=True), writes=["b_bq"])
        em.dma("P", lambda e: e.dma_start(out=Pd["b_bpr"][:, :], in_=din["bproj"]), writes=["b_bpr"])
        em.dma("P", lambda e: e.dma_start(out=Pd["b_bqv"][:, :], in_=din["bqkv"][0:1, 2048:3072]),
               writes=["b_bqv"])
        em.op("v", lambda e: e.memset(Pd["b_onr"][:, :], 1.0), writes=["b_onr"])
        for k in range(8):
            em.dma("Q", lambda e, k=k: e.dma_start(
                out=b_wqkv[:, k * 3072:(k + 1) * 3072],
                in_=bass.AP(wqkvT_bf.tensor, k * 128 * 3072, [[3072, 128], [1, 3072]])),
                reads=["wqkvbf"], writes=["b_wqkv"] + (["f_fd", "f_tu"] if k == 0 else []))
        em.op("v", lambda e: e.memset(b_vaug[:, :], 0.0),
              writes=["b_vaug", "f_ufm", "f_uq2"])
        em.op("v", lambda e: e.memset(
            bass.AP(b_vaug, 64, [[1040, 128], [65, 16], [1, 1]]), 1.0),
              reads=["b_vaug"], writes=["b_vaug"])

        NGB = 8
        NGB_RUN = int(_os.environ.get("NGB_RUN", str(NGB)))
        for g in range(NGB_RUN):
            for jj in range(2):
                j = g * 2 + jj
                rsl = a_resid[:, j * D:(j + 1) * D]
                emit_ln(rsl, ["resid"], f_xqb[:, :], ["f_xqb", "b_kp"], scr=b_scr1[:, :], scrkey="b_scr1")
                em.mm_group(
                    [lambda e, k=k: e.transpose(PB[0][:, k * 128:(k + 1) * 128],
                                                f_xqb[:, k * 128:(k + 1) * 128], ident[:, :])
                     for k in range(8)],
                    reads=["f_xqb", "ident"], writes=["pb0"])
                em.op("v", lambda e, jj=jj: e.tensor_copy(
                    bass.AP(b_xfm, jj * 128, [[2048, 128], [256, 8], [1, 128]]), PB[0][:, :]),
                      reads=["pb0"], writes=["b_xfm"] + (["f_res"] if g == 0 and jj == 0 else []))
            for ot in range(24):
                bank = PS[ot % 2]
                em.mm_group(
                    [lambda e, ot=ot, k=k, bank=bank: e.matmul(
                        bank[:, 0:256], b_wqkv[:, k * 3072 + ot * 128:k * 3072 + (ot + 1) * 128],
                        b_xfm[:, k * 256:(k + 1) * 256], start=(k == 0), stop=(k == 7))
                     for k in range(8)],
                    reads=["b_wqkv", "b_xfm"], writes=[f"ps{ot % 2}"])
                em.op("s", lambda e, ot=ot: e.activation(
                    b_qkv[:, ot * 256:(ot + 1) * 256], bank[:, 0:256] if False else PS[ot % 2][:, 0:256],
                    AF.Identity, bias=Pd["b_bq"][:, ot:ot + 1]),
                      reads=[f"ps{ot % 2}", "b_bq"],
                      writes=["b_qkv"] + (["gsil", "aS", "g1tpl", "ush"] if g == 0 and ot == 0 else []))
            for jj in range(2):
                j = g * 2 + jj
                SKIP_VT = _os.environ.get("SKIP_VT") == "1"
                SKIP_KF = _os.environ.get("SKIP_KF") == "1"
                SKIP_KV = _os.environ.get("SKIP_KV") == "1"
                fns = []
                for k in range(8):
                    xs = bass.AP(b_xfm, jj * 128, [[2048, 128], [1, 128]], None)
                    xs = b_xfm[:, k * 256 + jj * 128:k * 256 + (jj + 1) * 128]
                    for osp in range(2):
                        fns.append(lambda e, xs=xs, k=k, osp=osp: e.matmul(
                            PS[2 + osp][:, :], xs,
                            b_wqkv[:, k * 3072 + 2048 + osp * 512:k * 3072 + 2048 + (osp + 1) * 512],
                            start=(k == 0), stop=False, skip_group_check=True))
                for osp in range(2):
                    fns.append(lambda e, osp=osp, jj=jj: e.matmul(
                        PS[2 + osp][:, :], Pd["b_onr"][0:1, jj * 128:(jj + 1) * 128],
                        Pd["b_bqv"][0:1, osp * 512:(osp + 1) * 512],
                        start=False, stop=True, skip_group_check=True))
                if not SKIP_VT:
                    em.mm_group(fns, reads=["b_xfm", "b_wqkv", "b_onr", "b_bqv"],
                                writes=["ps2", "ps3"])
                for osp in range(2 if not SKIP_VT else 0):
                    em.op("v", lambda e, osp=osp: e.tensor_copy(
                        bass.AP(b_vaug, osp * 8 * 65, [[1040, 128], [65, 8], [1, 64]]),
                        PS[2 + osp][:, :].rearrange("p (h v) -> p h v", v=64)),
                          reads=[f"ps{2 + osp}", "b_vaug"], writes=["b_vaug"])
                fns = []
                for p in range(8):
                    lhs = b_qkv[:, (8 + p) * 256 + jj * 128:(8 + p) * 256 + (jj + 1) * 128]
                    bank = PS[2 + p // 4]
                    col = (p % 4) * 128
                    fns.append(lambda e, lhs=lhs, bank=bank, col=col: e.matmul(
                        bank[:, col:col + 128], lhs, Pd["b_pm2"][:, :],
                        start=True, stop=True, skip_group_check=True))
                if not SKIP_KF:
                    em.mm_group(fns, reads=["b_qkv", "b_pm2", "b_vaug"], writes=["ps2", "ps3"])
                for half in range(2 if not SKIP_KF else 0):
                    em.op("v", lambda e, half=half: e.reduce_max(
                        small[:, 32 + half * 8:40 + half * 8],
                        PS[2 + half][:, 0:512].rearrange("p (h f) -> p h f", f=64), axis=AX.X),
                          reads=[f"ps{2 + half}"], writes=[f"bmx{half}"])
                    mb = small[:, 32 + half * 8:40 + half * 8].unsqueeze(2).broadcast_to([128, 8, 64])
                    em.op("v", lambda e, half=half, mb=mb: e.tensor_tensor(
                        b_scr1[:, half * 512:(half + 1) * 512].rearrange("p (h f) -> p h f", f=64),
                        PS[2 + half][:, 0:512].rearrange("p (h f) -> p h f", f=64), mb, ALU.subtract),
                          reads=[f"ps{2 + half}", f"bmx{half}"], writes=["b_scr1"])
                if not SKIP_KF:
                    em.op("s", lambda e: e.activation(b_kp[:, :], b_scr1[:, :], AF.Exp, bias=1e-4),
                          reads=["b_scr1"], writes=["b_kp", "f_xqb"])
                if _os.environ.get("SKIP_QP") != "1":
                    fns = []
                    for p in range(8):
                        rhs = b_qkv[:, p * 256 + jj * 128:p * 256 + (jj + 1) * 128]
                        bank = PS[2 + p // 4]
                        col = (p % 4) * 128
                        fns.append(lambda e, rhs=rhs, bank=bank, col=col: e.matmul(
                            bank[:, col:col + 128], Pd["b_pm2"][:, :], rhs,
                            start=True, stop=True, skip_group_check=True))
                    em.mm_group(fns, reads=["b_pm2", "b_qkv", "b_kp"], writes=["ps2", "ps3"])
                    for p in range(8):
                        bank = PS[2 + p // 4]
                        col = (p % 4) * 128
                        dst = b_qp[:, p * 2048 + g * 256 + jj * 128:
                                   p * 2048 + g * 256 + (jj + 1) * 128]
                        em.op("s", lambda e, dst=dst, bank=bank, col=col: e.activation(
                            dst, bank[:, col:col + 128], AF.Exp, bias=1e-4),
                              reads=[f"ps{2 + p // 4}"],
                              writes=["b_qp"] + (["f_fd", "f_xq8"] if g == 0 and p == 0 and jj == 0 else []))
                if "d_kp" in dbg_out and g == 0 and jj == 0:
                    em.dma("Q", lambda e: e.dma_start(out=dbg_out["d_kp"], in_=b_kp[:, :]),
                           reads=["b_kp"], writes=["dkp"])
                    em.dma("Q", lambda e: e.dma_start(out=dbg_out["d_vaug"], in_=b_vaug[:, :]),
                           reads=["b_vaug"], writes=["dvaug"])
                fns = []
                for h in range(16):
                    p = h // 2
                    bank = PS[4 + p // 4]
                    col = (p % 4) * 128
                    base = 64 * (h % 2)
                    first = (g == 0 and jj == 0 and base == 0)
                    fns.append(lambda e, h=h, p=p, bank=bank, col=col, base=base, g=g, jj=jj, first=first: e.matmul(
                        bank[base:base + 65 if False else base + 64, col:col + 65][0:64, :] if False else bank[base:base + 64, col:col + 65],
                        b_kp[:, h * 64:(h + 1) * 64],
                        b_vaug[:, p * 65 if False else h * 65:h * 65 + 65][:, 0:65] if False else b_vaug[:, h * 65:h * 65 + 65],
                        start=(g == 0 and jj == 0 and h in (0, 1, 8, 9)),
                        stop=(g == NGB_RUN - 1 and jj == 1),
                        skip_group_check=True, tile_position=(0, base)))
                if not SKIP_KV:
                    em.mm_group(fns, reads=["b_kp", "b_vaug"], writes=["ps4", "ps5"])
        if "d_kv" in dbg_out:
            pass
        for bk in range(2):
            em.op("v", lambda e, bk=bk: e.tensor_copy(
                b_kvs[:, bk * 260:(bk + 1) * 260].rearrange("p (s c) -> p s c", c=65),
                PS[4 + bk][:, :].rearrange("p (s c) -> p s c", c=128)[:, :, 0:65]),
                  reads=[f"ps{4 + bk}"], writes=["b_kvs"])
        em.dma("Q", lambda e: e.dma_start(out=kv_in, in_=b_kvs[:, :]),
               reads=["b_kvs"], writes=["kv_in"])
        if _os.environ.get("SKIP_KVAR") != "1":
            em.cc(lambda e: e.collective_compute(
                "AllReduce", ALU.add, replica_groups=GROUPS, ins=[kv_in.opt()], outs=[kv_out.opt()]),
                reads=["kv_in"], writes=["kv_out"])
            em.dma("Q", lambda e: e.dma_start(out=b_kvs[:, :], in_=kv_out),
                   reads=["kv_out"], writes=["b_kvs"])
        em.op("v", lambda e: e.tensor_copy(b_kvb[:, :], b_kvs[:, :]),
              reads=["b_kvs"], writes=["b_kvb"])
        if "d_kv" in dbg_out:
            em.dma("Q", lambda e: e.dma_start(out=dbg_out["d_kv"], in_=b_kvs[:, :]),
                   reads=["b_kvs"], writes=["dkv"])
        em.op("v", lambda e: e.memset(b_kvb2[:, :], 0.0),
              reads=(["dkv"] if "d_kv" in dbg_out else []), writes=["b_kvb2", "b_kvs"])
        for p in range(8):
            em.op("v", lambda e, p=p: e.tensor_copy(b_kvb2[0:64, p * 128:p * 128 + 64],
                                                    b_kvb[0:64, p * 65:p * 65 + 64]),
                  reads=["b_kvb", "b_kvb2"], writes=["b_kvb2"])
            em.op("v", lambda e, p=p: e.tensor_copy(b_kvb2[64:128, p * 128 + 64:p * 128 + 128],
                                                    b_kvb[64:128, p * 65:p * 65 + 64]),
                  reads=["b_kvb", "b_kvb2"], writes=["b_kvb2"])
        if "d_qp" in dbg_out:
            em.dma("Q", lambda e: e.dma_start(out=dbg_out["d_qp"], in_=b_qp[:, :]),
                   reads=["b_qp"], writes=["dqp"])
        em.op("v", lambda e: e.memset(b_ksm8[:, :], 0.0), writes=["b_ksm8"])
        for h in range(16):
            p = h // 2
            base = 64 * (h % 2)
            em.op("v", lambda e, h=h, p=p, base=base: e.tensor_copy(
                b_ksm8[base:base + 64, 16 * p + h % 2 * 1 + 0:0] if False else
                b_ksm8[base:base + 64, 16 * p + h:16 * p + h + 1],
                b_kvb[base:base + 64, p * 65 + 64:p * 65 + 65]),
                  reads=["b_kvb", "b_ksm8"], writes=["b_ksm8"])
        for k in range(8):
            em.dma("Q", lambda e, k=k: e.dma_start(
                out=b_wqkv[:, k * 1024:(k + 1) * 1024],
                in_=bass.AP(wprojT_bf.tensor, k * 128 * 1024, [[1024, 128], [1, 1024]])),
                reads=["wprojbf"], writes=["b_wqkv"])
        for g in range(NGB if NGB_RUN == NGB else 0):
            em.mm_group(
                [lambda e, p=p, g=g: e.matmul(
                    PS[2][0:16, 0:256], b_ksm8[:, 16 * p:16 * (p + 1)],
                    b_qp[:, p * 2048 + g * 256:p * 2048 + (g + 1) * 256],
                    start=(p == 0), stop=(p == 7)) for p in range(8)],
                reads=["b_ksm8", "b_qp"], writes=["ps2"])
            em.op("v", lambda e: e.tensor_single_scalar(b_scr1[0:16, 0:256], PS[2][0:16, 0:256],
                                                        1e-6, ALU.add),
                  reads=["ps2"], writes=["b_scr1"])
            em.op("v", lambda e: e.reciprocal(b_scr1[0:16, 256:512], b_scr1[0:16, 0:256]),
                  reads=["b_scr1"], writes=["b_scr1"])
            em.op("v", lambda e: e.tensor_copy(b_rn[:, :], b_scr1[0:16, 256:512]),
                  reads=["b_scr1"], writes=["b_rn"])
            for p in range(8):
                em.mm_group([lambda e, p=p: e.matmul(
                    PS[1][:, 0:256], Pd["b_sel"][0:16, p * 128:(p + 1) * 128], b_rn[:, :],
                    start=True, stop=True)],
                    reads=["b_sel", "b_rn"], writes=["ps1"])
                em.op("v", lambda e: e.tensor_copy(f_xqb[:, 0:256], PS[1][:, 0:256]),
                      reads=["ps1"], writes=["f_xqb", "b_kp"])
                em.mm_group(
                    [lambda e, p=p, g=g: e.matmul(
                        PS[0][:, 0:256], b_kvb2[:, p * 128:(p + 1) * 128],
                        b_qp[:, p * 2048 + g * 256:p * 2048 + (g + 1) * 256],
                        start=True, stop=True, skip_group_check=True)],
                    reads=["b_kvb2", "b_qp", "f_xqb"], writes=["ps0"])
                em.op("v", lambda e, p=p: e.tensor_tensor(
                    b_osc[:, p * 256:(p + 1) * 256], PS[0][:, 0:256], f_xqb[:, 0:256], ALU.mult),
                      reads=["ps0", "f_xqb"], writes=["b_osc"])
            for ot in range(8):
                bank = PS[3]
                fns = [lambda e, ot=ot, k=k: e.matmul(
                    bank[:, 0:256], b_wqkv[:, k * 1024 + ot * 128:k * 1024 + (ot + 1) * 128],
                    b_osc[:, k * 256:(k + 1) * 256], start=(k == 0), stop=False,
                    skip_group_check=True) for k in range(8)]
                fns.append(lambda e, ot=ot: e.matmul(
                    bank[:, 0:256], Pd["b_bpr"][0:1, ot * 128:(ot + 1) * 128],
                    Pd["b_onr"][0:1, 0:256], start=False, stop=True, skip_group_check=True))
                em.mm_group(fns, reads=["b_wqkv", "b_osc", "b_bpr", "b_onr"], writes=["ps3"])
                em.op("v", lambda e, ot=ot: e.tensor_copy(b_prj[:, ot * 256:(ot + 1) * 256],
                                                          PS[3][:, 0:256]),
                      reads=["ps3"], writes=["b_prj"])
            if "d_osc" in dbg_out and g == 0:
                em.dma("Q", lambda e: e.dma_start(out=dbg_out["d_osc"], in_=b_osc[:, :]),
                       reads=["b_osc"], writes=["dosc"])
                em.dma("Q", lambda e: e.dma_start(out=dbg_out["d_prj"], in_=b_prj[:, :]),
                       reads=["b_prj"], writes=["dprj"])
            for jj in range(2):
                j = g * 2 + jj
                em.mm_group(
                    [lambda e, k=k, jj=jj: e.transpose(
                        PB[0][:, k * 128:(k + 1) * 128],
                        b_prj[:, k * 256 + jj * 128:k * 256 + (jj + 1) * 128], ident[:, :])
                     for k in range(8)],
                    reads=["b_prj", "ident"], writes=["pb0"])
                for k in range(8):
                    em.op("v", lambda e, j=j, k=k: e.tensor_tensor(
                        a_resid[:, j * D + k * 128:j * D + (k + 1) * 128],
                        a_resid[:, j * D + k * 128:j * D + (k + 1) * 128],
                        PB[0][:, k * 128:(k + 1) * 128], ALU.add),
                          reads=["pb0", "resid"], writes=["resid"])

    emit_attnb()
    if "d_res3" in dbg_out:
        em.dma("Q", lambda e: e.dma_start(out=dbg_out["d_res3"], in_=a_resid[:, :]),
               reads=["resid"], writes=["dres3"])

    if _os.environ.get("SKIP_FFN2") != "1":
        emit_ffn(1)


    if "d_c" in dbg_out:
        em.dma("Q", lambda e: e.dma_start(out=dbg_out["d_c"], in_=a_p1[:, :]),
               reads=["p1s"], writes=["dc"])
        em.dma("Q", lambda e: e.dma_start(out=dbg_out["d_tot"], in_=a_tot[:, :]),
               reads=["tot"], writes=["dtot"])
        em.dma("Q", lambda e: e.dma_start(out=dbg_out["d_thr"], in_=a_thr[:, :]),
               reads=["thrbc"], writes=["dthr"])
        em.dma("Q", lambda e: e.dma_start(out=dbg_out["d_orow"], in_=a_orow[:, :]),
               reads=["orow"], writes=["dorow"])
    if "d_keep" in dbg_out:
        em.dma("Q", lambda e: e.dma_start(out=dbg_out["d_keep"], in_=keep_all[:, :]),
               reads=["keep"], writes=["dk"])
        em.dma("Q", lambda e: e.dma_start(out=dbg_out["d_u"], in_=u_all[:, :]),
               reads=["u_all"], writes=["du"])
        em.dma("Q", lambda e: e.dma_start(out=dbg_out["d_lgg"], in_=a_lgg[:, :]),
               reads=["lgg"], writes=["dlg"])
        em.dma("Q", lambda e: e.dma_start(out=dbg_out["d_ag1"], in_=ag1_out),
               reads=["ag1_out"], writes=["dag1"])

    for j in range(NCH_L):
        em.dma("Q", lambda e, j=j: e.dma_start(
            out=OUT[j * 128:(j + 1) * 128, :], in_=a_resid[:, j * D:(j + 1) * D]),
            reads=["resid"], writes=[f"outw{j}"])

    em.finalize()
    return nc, dbg_out


def kernel(**inputs):
    in_maps = make_in_maps(inputs)
    nc, _ = build_program()
    res = run_bass_kernel_spmd(nc, in_maps, core_ids=list(range(NCORE)))
    out = np.zeros((B, S, D), np.float32)
    for c in range(NCORE):
        b, r = c // 4, c % 4
        out[b, r * TPC:(r + 1) * TPC] = np.asarray(res.results[c]["out"], np.float32)
    return out
